# version 01 (final)
# baseline (speedup 1.0000x reference)
import numpy as np
import ml_dtypes
from contextlib import ExitStack
import concourse.bass as bass
import concourse.mybir as mybir
from concourse.bass_utils import run_bass_kernel_spmd

F32 = mybir.dt.float32
BF16 = mybir.dt.bfloat16
AF = mybir.ActivationFunctionType
ALU = mybir.AluOpType
AX = mybir.AxisListType

P = 128
NC = 8
B = 2
T = 8192
D = 1024
TW = T // NC
NT = TW // P
NTILES = B * NT
S = 64
PADW = TW + 2 * P
NPT = PADW // P
EPS = 1e-5

ENGINES = ("sync", "vector", "scalar", "tensor", "gpsimd")


class Prog:
    def __init__(self, nc, ctx):
        import collections
        self.nc = nc
        self.ctx = ctx
        self.plan = {e: [] for e in ENGINES}
        self.sems = {}
        self.cnt = collections.defaultdict(int)

    def sem(self, name):
        if name not in self.sems:
            self.sems[name] = self.ctx.enter_context(self.nc.semaphore(name))
            self.cnt[name] = 0
        return self.sems[name]

    def bump(self, name, k):
        self.sem(name)
        self.cnt[name] += k
        return self.cnt[name]

    CHAINED = ("vector", "scalar", "gpsimd")

    def do(self, eng, fn, wait=(), inc=None, kind="compute"):
        for s, _ in wait:
            self.sem(s)
        wait = list(wait)
        last = getattr(self, "last", None)
        if last is None:
            last = self.last = {}
        if kind == "compute" and eng in self.CHAINED:
            prev = last.get(eng)
            if prev is not None:
                wait.append(prev)
            if inc is None:
                ch = eng + "_ch"
                self.bump(ch, 1)
                inc = None
                isem, ival = self.sems[ch], 1
                last[eng] = (ch, self.cnt[ch])
            else:
                sname, k = inc
                self.bump(sname, k)
                isem, ival = self.sems[sname], k
                last[eng] = (sname, self.cnt[sname])
                inc = None
        else:
            if inc is not None:
                sname, k = inc
                self.bump(sname, k)
                isem, ival = self.sems[sname], k
            else:
                isem, ival = None, 0
        waits = [(self.sems[s], v) for s, v in wait if v > 0]

        def closure(proxy, waits=waits, fn=fn, isem=isem, ival=ival):
            for s, v in waits:
                proxy.wait_ge(s, v)
            instr = fn(proxy)
            if isem is not None:
                instr.then_inc(isem, ival)

        self.plan[eng].append(closure)

    def wait(self, eng, sem, val):
        s = self.sem(sem)
        if val > 0:
            self.plan[eng].append(lambda proxy, s=s, val=val: proxy.wait_ge(s, val))

    def finalize(self):
        block = self.ctx.enter_context(self.nc.Block())

        @block.sync
        def _(e):
            for f in self.plan["sync"]:
                f(e)

        @block.vector
        def _(e):
            for f in self.plan["vector"]:
                f(e)

        @block.scalar
        def _(e):
            for f in self.plan["scalar"]:
                f(e)

        @block.tensor
        def _(e):
            for f in self.plan["tensor"]:
                f(e)

        @block.gpsimd
        def _(e):
            for f in self.plan["gpsimd"]:
                f(e)


class PsumPool:
    def __init__(self, p, ctx, name, shape, dtype, nbuf):
        self.p = p
        self.tiles = [ctx.enter_context(p.nc.psum_tensor(f"{name}{i}", shape, dtype))
                      for i in range(nbuf)]
        self.nbuf = nbuf
        self.g = 0
        self.mm = name + "_mm"
        self.ev = name + "_ev"
        p.sem(self.mm)
        p.sem(self.ev)

    def group(self, matmuls, evac, evac_eng="vector", pre_wait=(), evac_wait=()):
        p = self.p
        g = self.g
        tile = self.tiles[g % self.nbuf]
        waits = list(pre_wait)
        if g >= self.nbuf:
            waits.append((self.ev, g - self.nbuf + 1))
        n = len(matmuls)
        for i, mfn in enumerate(matmuls):
            p.do("tensor", lambda e, mfn=mfn, tile=tile: mfn(e, tile),
                 wait=waits if i == 0 else (),
                 inc=(self.mm, 1) if i == n - 1 else None)
        p.do(evac_eng, lambda e, evac=evac, tile=tile: evac(e, tile),
             wait=[(self.mm, p.cnt[self.mm])] + list(evac_wait), inc=(self.ev, 1))
        self.g += 1


def ln_emit(p, x_ap, out_fn, scr, extra=None):
    p.do("vector", lambda e: e.tensor_reduce(scr["red"][:], x_ap, AX.X, ALU.add),
         wait=list(extra or ()))
    p.do("vector", lambda e: e.tensor_single_scalar(scr["mneg"][:], scr["red"][:],
                                                    -1.0 / D, ALU.mult), inc=("vp", 1))
    p.do("scalar", lambda e: e.activation(scr["sq"][:], x_ap, AF.Square,
                                          bias=scr["mneg"][:], accum_out=scr["ssum"][:]),
         wait=[("vp", p.cnt["vp"])])
    p.do("scalar", lambda e: e.activation(scr["srt"][:], scr["ssum"][:], AF.Sqrt,
                                          bias=scr["epsc"][:], scale=1.0 / D),
         wait=[("cst", p.cnt["cst"])], inc=("sp", 1))
    p.do("vector", lambda e: e.reciprocal(scr["rstd"][:], scr["srt"][:]),
         wait=[("sp", p.cnt["sp"])])
    out_fn(p, scr["mneg"], scr["rstd"])


def alloc_ln_scratch(nc, E, p):
    scr = {k: E(nc.sbuf_tensor("lnscr_" + k, [P, 1], F32))
           for k in ("mneg", "ssum", "srt", "rstd", "red", "epsc")}
    scr["sq"] = E(nc.sbuf_tensor("lnscr_sq", [P, D], F32))
    p.do("vector", lambda e: e.memset(scr["epsc"][:], EPS), inc=("cst", 1))
    return scr


def emit_transposes(p, xnb, xnT, dest_cols, ptr, idb, tr_state):
    vt = p.cnt["vp"]
    for ic in range(8):
        tr = tr_state[0]
        pt_tile = ptr[tr % 2]
        p.do("tensor", lambda e, ic=ic, pt_tile=pt_tile: e.transpose(
            pt_tile[:], xnb[:, ic * P:(ic + 1) * P], idb[:]),
            wait=[("vp", vt), ("tp_ev", max(0, tr - 1))], inc=("tp", 1))
        p.do("vector", lambda e, ic=ic, pt_tile=pt_tile, dest_cols=dest_cols:
             e.tensor_copy(xnT[:, ic, dest_cols[0]:dest_cols[1]], pt_tile[:]),
             wait=[("tp", p.cnt["tp"])], inc=("tp_ev", 1))
        tr_state[0] += 1


def emit_top2(p, vals, scr8, out_mask):
    eq, cum, zeros8, f1, f2, m1 = (scr8[k] for k in ("eq", "cum", "z8", "f1", "f2", "m1"))
    p.do("vector", lambda e: e.tensor_reduce(m1[:], vals[:], AX.X, ALU.max))
    p.do("vector", lambda e: e.tensor_scalar(eq[:], vals[:], m1[:], None, ALU.is_equal))
    p.do("vector", lambda e: e.tensor_tensor_scan(cum[:], eq[:], zeros8[:], 0.0,
                                                  ALU.add, ALU.add))
    p.do("vector", lambda e: e.tensor_tensor(cum[:], cum[:], eq[:], ALU.subtract))
    p.do("vector", lambda e: e.tensor_single_scalar(cum[:], cum[:], 0.0, ALU.is_equal))
    p.do("vector", lambda e: e.tensor_tensor(f1[:], eq[:], cum[:], ALU.mult))
    p.do("vector", lambda e: e.scalar_tensor_tensor(out_mask[:], f1[:], -1e30, vals[:],
                                                    ALU.mult, ALU.add))
    p.do("vector", lambda e: e.tensor_reduce(m1[:], out_mask[:], AX.X, ALU.max))
    p.do("vector", lambda e: e.tensor_scalar(eq[:], out_mask[:], m1[:], None, ALU.is_equal))
    p.do("vector", lambda e: e.tensor_tensor_scan(cum[:], eq[:], zeros8[:], 0.0,
                                                  ALU.add, ALU.add))
    p.do("vector", lambda e: e.tensor_tensor(cum[:], cum[:], eq[:], ALU.subtract))
    p.do("vector", lambda e: e.tensor_single_scalar(cum[:], cum[:], 0.0, ALU.is_equal))
    p.do("vector", lambda e: e.tensor_tensor(f2[:], eq[:], cum[:], ALU.mult))
    p.do("vector", lambda e: e.tensor_tensor(out_mask[:], f1[:], f2[:], ALU.add))


def alloc_top2_scratch(nc, E):
    return {"eq": E(nc.sbuf_tensor("t2_eq", [P, 8], F32)),
            "cum": E(nc.sbuf_tensor("t2_cum", [P, 8], F32)),
            "z8": E(nc.sbuf_tensor("t2_z8", [P, 8], F32)),
            "f1": E(nc.sbuf_tensor("t2_f1", [P, 8], F32)),
            "f2": E(nc.sbuf_tensor("t2_f2", [P, 8], F32)),
            "m1": E(nc.sbuf_tensor("t2_m1", [P, 1], F32))}


def build_L1():
    nc = bass.Bass("TRN2", target_bir_lowering=False, debug=False)
    dt = nc.dram_tensor
    xpad = dt("xpad", [B, PADW, D], F32, kind="ExternalInput").ap()
    maskl = dt("maskl", [P, 1], F32, kind="ExternalInput").ap()
    maskr = dt("maskr", [P, 1], F32, kind="ExternalInput").ap()
    identb = dt("identb", [P, P], BF16, kind="ExternalInput").ap()
    nwb = dt("nwb", [P, D], F32, kind="ExternalInput").ap()
    nbb = dt("nbb", [P, D], F32, kind="ExternalInput").ap()
    inwT = dt("inwT", [D, 2 * D], F32, kind="ExternalInput").ap()
    inbc = dt("inbc", [P, 16], F32, kind="ExternalInput").ap()
    convw = dt("convw", [P, 8, 3], F32, kind="ExternalInput").ap()
    convb = dt("convb", [P, 8], F32, kind="ExternalInput").ap()
    dfm = dt("dfm", [P, 8], F32, kind="ExternalInput").ap()
    dtwT = dt("dtwT", [D, S], F32, kind="ExternalInput").ap()
    bwT = dt("bwT", [D, S], F32, kind="ExternalInput").ap()
    cwT = dt("cwT", [D, S], F32, kind="ExternalInput").ap()
    dtb = dt("dtb", [S, 1], F32, kind="ExternalInput").ap()
    bbias = dt("bbias", [S, 1], F32, kind="ExternalInput").ap()
    cbias = dt("cbias", [S, 1], F32, kind="ExternalInput").ap()
    alog = dt("alog", [S, 1], F32, kind="ExternalInput").ap()
    outwT1 = dt("outwT1", [D, D], F32, kind="ExternalInput").ap()
    outb = dt("outb", [P, D], F32, kind="ExternalInput").ap()

    resid = dt("resid", [B, TW, D], F32, kind="ExternalOutput").ap()
    projp = dt("projp", [B, TW, D], F32, kind="ExternalOutput").ap()
    deco = dt("deco", [B, S, TW], F32, kind="ExternalOutput").ap()
    bbo = dt("bbo", [B, S, TW], F32, kind="ExternalOutput").ap()
    cco = dt("cco", [B, S, TW], F32, kind="ExternalOutput").ap()

    with ExitStack() as ctx:
        E = ctx.enter_context
        p = Prog(nc, ctx)

        xt = E(nc.sbuf_tensor("xt", [P, 1, D], F32))
        xn0 = E(nc.sbuf_tensor("xn0", [P, D], F32))
        xnb = E(nc.sbuf_tensor("xnb", [P, D], BF16))
        scr = alloc_ln_scratch(nc, E, p)
        xn1 = scr["sq"]
        xnT = E(nc.sbuf_tensor("xnT", [P, 8, 2 * PADW], BF16))
        idb = E(nc.sbuf_tensor("idb", [P, P], BF16))
        nwt = E(nc.sbuf_tensor("nwt", [P, D], F32))
        nbt = E(nc.sbuf_tensor("nbt", [P, D], F32))
        mlt = E(nc.sbuf_tensor("mlt", [P, 1], F32))
        mrt = E(nc.sbuf_tensor("mrt", [P, 1], F32))
        wstg = E(nc.sbuf_tensor("wstg", [P, 1024], F32))
        inwb = E(nc.sbuf_tensor("inwb", [P, 8, 2048], BF16))
        ow1b = E(nc.sbuf_tensor("ow1b", [P, 8, D], BF16))
        pwb = E(nc.sbuf_tensor("pwb", [P, 8, 3, S], BF16))
        cwt = E(nc.sbuf_tensor("cwt", [P, 8, 3], F32))
        cbt = E(nc.sbuf_tensor("cbt", [P, 8], F32))
        dft = E(nc.sbuf_tensor("dft", [P, 8], F32))
        pbias = E(nc.sbuf_tensor("pbias", [S, 3], F32))
        alogt = E(nc.sbuf_tensor("alogt", [S, 1], F32))
        acol = E(nc.sbuf_tensor("acol", [S, 1], F32))
        aexp = E(nc.sbuf_tensor("aexp", [S, 1], F32))
        obt = E(nc.sbuf_tensor("obt", [P, D], F32))
        inbt = E(nc.sbuf_tensor("inbt", [P, 16], F32))
        ph12 = E(nc.sbuf_tensor("ph12", [P, 2 * PADW], F32))
        xp = ph12
        cvt = E(nc.sbuf_tensor("cvt", [P, TW], F32))
        xcd = E(nc.sbuf_tensor("xcd", [P, 8, 2 * TW], BF16))
        zbf = E(nc.sbuf_tensor("zbf", [P, 8, 2 * TW], BF16))
        uscr = E(nc.sbuf_tensor("uscr", [S, 512], F32))
        rscr = E(nc.sbuf_tensor("rscr", [S, 512], F32))
        ascr = E(nc.sbuf_tensor("ascr", [S, 512], F32))
        escr = ascr
        dtv = uscr
        dlv = rscr
        obuf = E(nc.sbuf_tensor("obuf", [S, 1, 512], F32))
        onec = alogt
        prj = cvt

        pmm = PsumPool(p, ctx, "pmm", [P, 512], F32, 4)
        ptr = [E(nc.psum_tensor(f"ptr{i}", [P, P], BF16)) for i in range(2)]
        pps = PsumPool(p, ctx, "pps", [S, 512], F32, 2)

        def ldw(dst_ap, src_ap, wait=(), sem="ins"):
            p.do("sync", lambda e: e.dma_start(out=dst_ap, in_=src_ap),
                 wait=wait, inc=(sem, 16))

        for dst, src in ((idb[:], identb[:]), (nwt[:], nwb[:]), (nbt[:], nbb[:]),
                         (mlt[:], maskl[:]), (mrt[:], maskr[:]), (cwt[:], convw[:]),
                         (cbt[:], convb[:]), (dft[:], dfm[:]),
                         (pbias[:, 0:1], dtb[:]), (pbias[:, 1:2], bbias[:]),
                         (pbias[:, 2:3], cbias[:]), (alogt[:], alog[:]),
                         (obt[:], outb[:]), (inbt[:], inbc[:])):
            ldw(dst, src)
        ins_all = p.cnt["ins"]
        for eng in ("vector", "scalar", "tensor", "gpsimd"):
            p.wait(eng, "ins", ins_all)
        p.do("scalar", lambda e: e.activation(aexp[:], alogt[:], AF.Exp))
        p.do("scalar", lambda e: e.mul(acol[:], aexp[:], -1.0), inc=("sp", 1))
        p.do("vector", lambda e: e.memset(onec[:], 1.0),
             wait=[("sp", p.cnt["sp"])], inc=("cst", 1))

        wjobs = []
        for ic in range(8):
            for hh in range(2):
                wjobs.append((inwT[ic * P:(ic + 1) * P, hh * 1024:(hh + 1) * 1024],
                              inwb[:, ic, hh * 1024:(hh + 1) * 1024], 1024))
        for ic in range(8):
            wjobs.append((outwT1[ic * P:(ic + 1) * P, :], ow1b[:, ic, :], D))
        for ic in range(8):
            for j, w in enumerate((dtwT, bwT, cwT)):
                wjobs.append((w[ic * P:(ic + 1) * P, :], pwb[:, ic, j, :], S))
        for k, (src, dst, wdt) in enumerate(wjobs):
            ldw(wstg[:, 0:wdt], src, wait=[("wc", k)], sem="wst")
            p.do("gpsimd", lambda e, dst=dst, wdt=wdt: e.tensor_copy(
                dst, wstg[:, 0:wdt]),
                wait=[("wst", p.cnt["wst"])], inc=("wc", 1))
        w_all = p.cnt["wc"]

        tr_state = [0]
        n_load = 0
        for b in range(B):
            for pt in range(NPT):
                owned = 1 <= pt <= 8
                p.do("sync", lambda e, b=b, pt=pt: e.dma_start(
                    out=xt[:, 0, :], in_=xpad[b, pt * P:(pt + 1) * P, :]),
                    wait=[("xuse", n_load)], inc=("xld", 16))
                n_load += 1
                xa = xt[:, 0, :]
                in_t = p.cnt["xld"]

                def consume1(p_, mneg, rstd, xa=xa, owned=owned, b=b, pt=pt):
                    if owned:
                        oi = b * NT + (pt - 1)
                        rbuf = oi % 2
                        rview = ph12[:, rbuf * D:(rbuf + 1) * D]
                        p_.do("vector", lambda e: e.tensor_scalar(
                            xn0[:], xa, mneg[:], rstd[:], ALU.add, ALU.mult))
                        p_.do("vector", lambda e, rview=rview: e.tensor_tensor(
                            rview, xa, xn0[:], ALU.add),
                            wait=[("rout", oi * 16)],
                            inc=("xuse", 1))
                        p_.do("gpsimd", lambda e, b=b, pt=pt, rview=rview: e.dma_start(
                            out=resid[b, (pt - 1) * P:pt * P, :], in_=rview),
                            wait=[("xuse", p_.cnt["xuse"])], inc=("rout", 16))
                    else:
                        p_.do("vector", lambda e: e.tensor_scalar(
                            xn0[:], xa, mneg[:], rstd[:], ALU.add, ALU.mult),
                            inc=("xuse", 1))

                ln_emit(p, xa, consume1, scr, extra=[("xld", in_t)])

                def consume2(p_, mneg, rstd):
                    p_.do("vector", lambda e: e.tensor_scalar(
                        xn1[:], xn0[:], mneg[:], rstd[:], ALU.add, ALU.mult))
                    p_.do("vector", lambda e: e.tensor_tensor(xn1[:], xn1[:], nwt[:], ALU.mult))
                    p_.do("vector", lambda e: e.tensor_tensor(xnb[:], xn1[:], nbt[:], ALU.add),
                          wait=[("tp", p_.cnt["tp"])], inc=("vp", 1))

                ln_emit(p, xn0[:], consume2, scr)
                col = b * PADW + pt * P
                emit_transposes(p, xnb, xnT, (col, col + P), ptr, idb, tr_state)
        xnT_done = p.cnt["tp_ev"]

        pad_groups = []
        for b in range(B):
            base = b * PADW
            pad_groups += [(base, 512), (base + 512, 512), (base + 1024, 256)]
        own_groups = []
        for b in range(B):
            base = b * PADW + P
            own_groups += [(base, 512), (base + 512, 512)]

        for oc in range(16):
            is_xp = oc < 8
            groups = pad_groups if is_xp else own_groups
            for gi, (c0, w) in enumerate(groups):
                mms = []
                for ic in range(8):
                    mms.append(lambda e, tile, ic=ic, oc=oc, c0=c0, w=w:
                               e.matmul(tile[:, 0:w], inwb[:, ic, oc * P:(oc + 1) * P],
                                        xnT[:, ic, c0:c0 + w],
                                        start=(ic == 0), stop=(ic == 7)))
                if is_xp:
                    def ev(e, tile, c0=c0, w=w, oc=oc):
                        return e.tensor_scalar(xp[:, c0:c0 + w], tile[:, 0:w],
                                               inbt[:, oc:oc + 1], None, ALU.add)
                else:
                    b_i = 0 if c0 < PADW else 1
                    oc0 = (c0 - b_i * PADW - P) + b_i * TW
                    def ev(e, tile, oc=oc, oc0=oc0, w=w):
                        return e.tensor_scalar(zbf[:, oc - 8, oc0:oc0 + w], tile[:, 0:w],
                                               inbt[:, oc:oc + 1], None, ALU.add)
                pre = [("tp_ev", xnT_done), ("wc", w_all),
                       ("rout", p.cnt["rout"])] if (oc == 0 and gi == 0) else ()
                pmm.group(mms, ev, pre_wait=pre)
            if is_xp:
                vnow = p.cnt["pmm_ev"]
                for cpos, mt in ((P - 1, mlt), (PADW + P - 1, mlt),
                                 (PADW - P, mrt), (2 * PADW - P, mrt)):
                    p.do("vector", lambda e, cpos=cpos, mt=mt: e.tensor_tensor(
                        xp[:, cpos:cpos + 1], xp[:, cpos:cpos + 1], mt[:], ALU.mult),
                        wait=[("pmm_ev", vnow)] if cpos == P - 1 else ())
                for b in range(B):
                    src = b * PADW + P
                    p.do("vector", lambda e, src=src, oc=oc: e.tensor_scalar(
                        cvt[:], xp[:, src - 1:src - 1 + TW],
                        cwt[:, oc, 0:1], None, ALU.mult))
                    p.do("vector", lambda e, src=src, oc=oc: e.scalar_tensor_tensor(
                        cvt[:], xp[:, src:src + TW], cwt[:, oc, 1:2],
                        cvt[:], ALU.mult, ALU.add))
                    p.do("vector", lambda e, src=src, oc=oc: e.scalar_tensor_tensor(
                        cvt[:], xp[:, src + 1:src + 1 + TW], cwt[:, oc, 2:3],
                        cvt[:], ALU.mult, ALU.add), inc=("vp", 1))
                    p.do("scalar", lambda e, oc=oc: e.activation(
                        cvt[:], cvt[:], AF.Tanh, bias=cbt[:, oc:oc + 1]),
                        wait=[("vp", p.cnt["vp"])], inc=("sp", 1))
                    p.do("vector", lambda e, oc=oc, b=b: e.tensor_scalar(
                        xcd[:, oc, b * TW:(b + 1) * TW], cvt[:], dft[:, oc:oc + 1],
                        None, ALU.mult),
                        wait=[("sp", p.cnt["sp"])], inc=("conv", 1))
        z_ready = p.cnt["pmm_ev"]
        conv_all = p.cnt["conv"]

        own2 = [(b, g * 512) for b in range(B) for g in range(2)]
        kout = 0
        for j, tgt in enumerate((deco, bbo, cco)):
            for gi, (b_i, c0) in enumerate(own2):
                zc0 = b_i * TW + c0
                mms = []
                for ic in range(8):
                    mms.append(lambda e, tile, ic=ic, j=j, zc0=zc0:
                               e.matmul(tile[:], pwb[:, ic, j, :], zbf[:, ic, zc0:zc0 + 512],
                                        start=(ic == 0), stop=(ic == 7)))
                pre = [("pmm_ev", z_ready)] if (j == 0 and gi == 0) else ()
                oview = obuf[:, 0, :]
                if j == 0:
                    def ev(e, tile):
                        return e.tensor_scalar(uscr[:], tile[:], pbias[:, 0:1], None, ALU.add)
                    pps.group(mms, ev, pre_wait=pre,
                              evac_wait=[("sp", p.cnt["sp"])])
                    p.do("vector", lambda e: e.tensor_single_scalar(rscr[:], uscr[:], 0.0, ALU.max),
                         wait=[("pps_ev", p.cnt["pps_ev"]), ("dec", p.cnt["dec"])],
                         inc=("vp", 1))
                    p.do("scalar", lambda e: e.activation(ascr[:], uscr[:], AF.Abs),
                         wait=[("vp", p.cnt["vp"])])
                    p.do("scalar", lambda e: e.activation(escr[:], ascr[:], AF.Exp, scale=-1.0))
                    p.do("scalar", lambda e: e.activation(escr[:], escr[:], AF.Ln,
                                                          bias=onec[:]),
                         wait=[("cst", p.cnt["cst"])], inc=("sp", 1))
                    p.do("vector", lambda e: e.tensor_tensor(dtv[:], rscr[:], escr[:], ALU.add),
                         wait=[("sp", p.cnt["sp"])])
                    p.do("vector", lambda e: e.tensor_single_scalar(dtv[:], dtv[:], 1e-4, ALU.add))
                    p.do("vector", lambda e: e.tensor_scalar(dlv[:], dtv[:], acol[:], None,
                                                             ALU.mult), inc=("vp", 1))
                    p.do("scalar", lambda e, oview=oview: e.activation(
                        oview, dlv[:], AF.Exp),
                        wait=[("vp", p.cnt["vp"]), ("out", kout * 16)],
                        inc=("dec", 1))
                    p.do("gpsimd", lambda e, tgt=tgt, b_i=b_i, c0=c0, oview=oview: e.dma_start(
                        out=tgt[b_i, :, c0:c0 + 512], in_=oview),
                        wait=[("dec", p.cnt["dec"])], inc=("out", 16), kind="dma")
                else:
                    def ev(e, tile, oview=oview, j=j):
                        return e.activation(oview, tile[:], AF.Tanh,
                                            bias=pbias[:, j:j + 1])
                    pre = list(pre) + [("out", kout * 16)]
                    pps.group(mms, ev, evac_eng="scalar", pre_wait=pre)
                    p.do("gpsimd", lambda e, tgt=tgt, b_i=b_i, c0=c0, oview=oview: e.dma_start(
                        out=tgt[b_i, :, c0:c0 + 512], in_=oview),
                        wait=[("pps_ev", p.cnt["pps_ev"])], inc=("out", 16), kind="dma")
                kout += 1

        for b in range(B):
            for j in range(NT):
                k = b * NT + j
                for h in range(2):
                    mms = []
                    for ic in range(8):
                        mms.append(lambda e, tile, ic=ic, b=b, j=j, h=h:
                                   e.matmul(tile[:], xcd[:, ic, b * TW + j * P:b * TW + (j + 1) * P],
                                            ow1b[:, ic, h * 512:(h + 1) * 512],
                                            start=(ic == 0), stop=(ic == 7)))

                    def ev(e, tile, h=h):
                        return e.tensor_tensor(prj[:, h * 512:(h + 1) * 512],
                                               tile[:], obt[:, h * 512:(h + 1) * 512], ALU.add)
                    pre = []
                    if k == 0 and h == 0:
                        pre.append(("conv", conv_all))
                    if h == 0:
                        pre.append(("pout", k * 16))
                    pmm.group(mms, ev, pre_wait=pre)
                p.do("gpsimd", lambda e, b=b, j=j: e.dma_start(
                    out=projp[b, j * P:(j + 1) * P, :], in_=prj[:]),
                    wait=[("pmm_ev", p.cnt["pmm_ev"])], inc=("pout", 16), kind="dma")

        p.wait("gpsimd", "out", p.cnt["out"])
        p.wait("gpsimd", "rout", p.cnt["rout"])
        p.wait("gpsimd", "pout", p.cnt["pout"])
        p.finalize()
    return nc


def build_L2():
    nc = bass.Bass("TRN2", target_bir_lowering=False, debug=False)
    dt = nc.dram_tensor
    residi = dt("residi", [B, TW, D], F32, kind="ExternalInput").ap()
    projpi = dt("projpi", [B, TW, D], F32, kind="ExternalInput").ap()
    decf = dt("decf", [B, S, T], F32, kind="ExternalInput").ap()
    bbf = dt("bbf", [B, S, T], F32, kind="ExternalInput").ap()
    ccs = dt("ccs", [B, S, TW], F32, kind="ExternalInput").ap()
    outwT2 = dt("outwT2", [S, D], F32, kind="ExternalInput").ap()
    gatew = dt("gatew", [D, 8], F32, kind="ExternalInput").ap()
    gateb = dt("gateb", [8, 1], F32, kind="ExternalInput").ap()
    identb = dt("identb", [P, P], BF16, kind="ExternalInput").ap()

    x1o = dt("x1o", [B, TW, D], F32, kind="ExternalOutput").ap()
    xn2To = dt("xn2To", [8, P, B * TW], BF16, kind="ExternalOutput").ap()
    logito = dt("logito", [8, B * TW], F32, kind="ExternalOutput").ap()
    lstato = dt("lstato", [8, 4], F32, kind="ExternalOutput").ap()

    with ExitStack() as ctx:
        E = ctx.enter_context
        p = Prog(nc, ctx)

        dec = E(nc.sbuf_tensor("dec", [P, T], F32))
        bb = E(nc.sbuf_tensor("bb", [P, T], F32))
        st = E(nc.sbuf_tensor("st", [P, T], F32))
        cc = E(nc.sbuf_tensor("cc", [S, 2, TW], F32))
        stw = E(nc.sbuf_tensor("stw", [S, TW], F32))
        yb = E(nc.sbuf_tensor("yb", [S, 2, TW], BF16))
        ow2b = E(nc.sbuf_tensor("ow2b", [S, D], BF16))
        wstg = E(nc.sbuf_tensor("wstg", [P, 2048], F32))
        gwb = E(nc.sbuf_tensor("gwb", [P, 8, 8], BF16))
        gbt = E(nc.sbuf_tensor("gbt", [8, 1], F32))
        rt = E(nc.sbuf_tensor("rt", [P, 2, D], F32))
        pj = E(nc.sbuf_tensor("pj", [P, 2, D], F32))
        x1t = E(nc.sbuf_tensor("x1t", [P, 2, D], F32))
        xnb = E(nc.sbuf_tensor("xnb", [P, D], BF16))
        xn2T = E(nc.sbuf_tensor("xn2T", [P, 8, B * TW], BF16))
        idb = E(nc.sbuf_tensor("idb", [P, P], BF16))
        scr = alloc_ln_scratch(nc, E, p)
        lgt = E(nc.sbuf_tensor("lgt", [8, B * TW], F32))
        lm = E(nc.sbuf_tensor("lm", [8, 2], F32))
        lmneg = E(nc.sbuf_tensor("lmneg", [8, 1], F32))
        lz = E(nc.sbuf_tensor("lz", [8, 2], F32))
        lex = E(nc.sbuf_tensor("lex", [8, TW], F32))
        lst = E(nc.sbuf_tensor("lst", [8, 4], F32))

        pmm = PsumPool(p, ctx, "pmm", [P, 512], F32, 4)
        plg = PsumPool(p, ctx, "plg", [8, 512], F32, 2)
        ptr = [E(nc.psum_tensor(f"ptr{i}", [P, P], BF16)) for i in range(2)]

        def ldw(dst_ap, src_ap, wait=(), sem="ins"):
            p.do("sync", lambda e: e.dma_start(out=dst_ap, in_=src_ap),
                 wait=wait, inc=(sem, 16))

        ldw(idb[:], identb[:])
        ldw(dec[0:S, :], decf[0])
        ldw(dec[S:P, :], decf[1])
        ldw(bb[0:S, :], bbf[0])
        ldw(bb[S:P, :], bbf[1])
        ldw(cc[:, 0, :], ccs[0])
        ldw(cc[:, 1, :], ccs[1])
        ldw(gbt[:], gateb[:])
        ins_all = p.cnt["ins"]
        for eng in ("vector", "scalar", "tensor", "gpsimd"):
            p.wait(eng, "ins", ins_all)
        ldw(wstg[0:S, 0:D], outwT2[:], wait=[("wc", 0)], sem="wst")
        p.do("gpsimd", lambda e: e.tensor_copy(ow2b[:], wstg[0:S, 0:D]),
             wait=[("wst", p.cnt["wst"])], inc=("wc", 1))
        for ic in range(8):
            ldw(wstg[:, 1024:1032],
                gatew[ic * P:(ic + 1) * P, :], wait=[("wc", p.cnt["wc"])], sem="wst")
            p.do("gpsimd", lambda e, ic=ic: e.tensor_copy(
                gwb[:, ic, :], wstg[:, 1024:1032]),
                wait=[("wst", p.cnt["wst"])], inc=("wc", 1))
        w_all = p.cnt["wc"]

        p.do("vector", lambda e: e.tensor_tensor_scan(
            st[:], dec[:], bb[:], 0.0, ALU.mult, ALU.add))
        scan_ch = p.cnt["vector_ch"]
        p.do("gpsimd", lambda e: e.dma_start(
            out=stw[:], in_=st[S:P, T - TW:T]), wait=[("vector_ch", scan_ch)],
            inc=("stw", 16), kind="dma")
        p.do("vector", lambda e: e.tensor_tensor(
            yb[:, 0, :], cc[:, 0, :], st[0:S, T - TW:T], ALU.mult), inc=("y", 1))
        p.do("vector", lambda e: e.tensor_tensor(
            yb[:, 1, :], cc[:, 1, :], stw[:], ALU.mult),
            wait=[("stw", 16)], inc=("y", 1))
        y_done = p.cnt["y"]

        tr_state = [0]
        for b in range(B):
            for j in range(NT):
                ti = b * NT + j
                buf = ti % 2
                p.do("sync", lambda e, b=b, j=j, buf=buf: e.dma_start(
                    out=rt[:, buf, :], in_=residi[b, j * P:(j + 1) * P, :]),
                    wait=[("x1", ti)], inc=("rtl", 16))
                p.do("sync", lambda e, b=b, j=j, buf=buf: e.dma_start(
                    out=pj[:, buf, :], in_=projpi[b, j * P:(j + 1) * P, :]), inc=("rtl", 16))
                in_now = p.cnt["rtl"]
                for h in range(2):
                    mms = [lambda e, tile, b=b, j=j, h=h: e.matmul(
                        tile[:], yb[:, b, j * P:(j + 1) * P],
                        ow2b[:, h * 512:(h + 1) * 512], start=True, stop=True)]

                    def ev(e, tile, buf=buf, h=h):
                        return e.tensor_tensor(x1t[:, buf, h * 512:(h + 1) * 512], tile[:],
                                               rt[:, buf, h * 512:(h + 1) * 512], ALU.add)
                    pmm.group(mms, ev, pre_wait=[("y", y_done), ("rtl", in_now),
                                                 ("wc", 1), ("xout", ti * 16)])
                p.do("vector", lambda e, buf=buf: e.tensor_tensor(
                    x1t[:, buf, :], x1t[:, buf, :], pj[:, buf, :], ALU.add),
                    wait=[("pmm_ev", p.cnt["pmm_ev"])],
                    inc=("x1", 1))
                p.do("gpsimd", lambda e, b=b, j=j, buf=buf: e.dma_start(
                    out=x1o[b, j * P:(j + 1) * P, :], in_=x1t[:, buf, :]),
                    wait=[("x1", p.cnt["x1"])], inc=("xout", 16), kind="dma")

                def consume(p_, mneg, rstd, buf=buf):
                    p_.do("vector", lambda e: e.tensor_scalar(
                        xnb[:], x1t[:, buf, :], mneg[:], rstd[:], ALU.add, ALU.mult),
                        wait=[("tp", p_.cnt["tp"])], inc=("vp", 1))

                ln_emit(p, x1t[:, buf, :], consume, scr, extra=[("x1", p.cnt["x1"])])
                emit_transposes(p, xnb, xn2T, (ti * P, (ti + 1) * P), ptr, idb, tr_state)
        xn2T_done = p.cnt["tp_ev"]
        p.do("gpsimd", lambda e: e.dma_start(out=xn2To.rearrange("c p t -> p c t"),
                                             in_=xn2T[:]),
             wait=[("tp_ev", xn2T_done)], inc=("out", 16), kind="dma")

        for g in range(4):
            c0 = g * 512
            mms = []
            for ic in range(8):
                mms.append(lambda e, tile, ic=ic, c0=c0: e.matmul(
                    tile[:], gwb[:, ic, :], xn2T[:, ic, c0:c0 + 512],
                    start=(ic == 0), stop=(ic == 7)))

            def ev(e, tile, c0=c0):
                return e.tensor_scalar(lgt[:, c0:c0 + 512], tile[:], gbt[:], None, ALU.add)
            plg.group(mms, ev, pre_wait=[("tp_ev", xn2T_done), ("wc", w_all)])
        lg_done = p.cnt["plg_ev"]
        p.do("gpsimd", lambda e: e.dma_start(out=logito[:], in_=lgt[:]),
             wait=[("plg_ev", lg_done)], inc=("out", 16), kind="dma")
        for b in range(B):
            p.do("vector", lambda e, b=b: e.tensor_reduce(
                lm[:, b:b + 1], lgt[:, b * TW:(b + 1) * TW], AX.X, ALU.max),
                wait=[("plg_ev", lg_done)])
            p.do("vector", lambda e, b=b: e.tensor_single_scalar(
                lmneg[:], lm[:, b:b + 1], -1.0, ALU.mult), inc=("vp", 1))
            p.do("scalar", lambda e, b=b: e.activation(
                lex[:], lgt[:, b * TW:(b + 1) * TW], AF.Exp, bias=lmneg[:],
                accum_out=lz[:, b:b + 1]), wait=[("vp", p.cnt["vp"])], inc=("sp", 1))
            p.wait("vector", "sp", p.cnt["sp"])
        p.do("vector", lambda e: e.tensor_copy(lst[:, 0:1], lm[:, 0:1]),
             wait=[("sp", p.cnt["sp"])])
        p.do("vector", lambda e: e.tensor_copy(lst[:, 1:2], lz[:, 0:1]))
        p.do("vector", lambda e: e.tensor_copy(lst[:, 2:3], lm[:, 1:2]))
        p.do("vector", lambda e: e.tensor_copy(lst[:, 3:4], lz[:, 1:2]), inc=("vp", 1))
        p.do("gpsimd", lambda e: e.dma_start(out=lstato[:], in_=lst[:]),
             wait=[("vp", p.cnt["vp"])], inc=("out", 16), kind="dma")

        p.wait("gpsimd", "out", p.cnt["out"])
        p.wait("gpsimd", "xout", p.cnt["xout"])
        p.finalize()
    return nc


def build_L3():
    nc = bass.Bass("TRN2", target_bir_lowering=False, debug=False)
    dt = nc.dram_tensor
    xn2Ti = dt("xn2Ti", [8, P, B * TW], BF16, kind="ExternalInput").ap()
    logiti = dt("logiti", [8, B * TW], F32, kind="ExternalInput").ap()
    astat = dt("astat", [8, 4, NC], F32, kind="ExternalInput").ap()
    w1 = dt("w1", [8, D, D], F32, kind="ExternalInput").ap()
    w2 = dt("w2", [8, D, D], F32, kind="ExternalInput").ap()
    w3 = dt("w3", [8, D, D], F32, kind="ExternalInput").ap()
    b1c = dt("b1c", [P, 8, 8], F32, kind="ExternalInput").ap()
    b2c = dt("b2c", [P, 8, 8], F32, kind="ExternalInput").ap()
    b3r = dt("b3r", [8, D], F32, kind="ExternalInput").ap()
    i8 = dt("i8", [8, 8], F32, kind="ExternalInput").ap()

    vparto = dt("vparto", [8, D], F32, kind="ExternalOutput").ap()
    indo = dt("indo", [B * TW, 8], F32, kind="ExternalOutput").ap()

    with ExitStack() as ctx:
        E = ctx.enter_context
        p = Prog(nc, ctx)

        xn2T = E(nc.sbuf_tensor("xn2T", [P, 8, B * TW], BF16))
        lgt = E(nc.sbuf_tensor("lgt", [8, B * TW], F32))
        ast = E(nc.sbuf_tensor("ast", [8, 4, NC], F32))
        i8t = E(nc.sbuf_tensor("i8t", [8, 8], F32))
        Mneg = E(nc.sbuf_tensor("Mneg", [8, 2], F32))
        Zr = E(nc.sbuf_tensor("Zr", [8, 2], F32))
        mg = E(nc.sbuf_tensor("mg", [8, 1], F32))
        dsc = E(nc.sbuf_tensor("dsc", [8, NC], F32))
        esc = E(nc.sbuf_tensor("esc", [8, NC], F32))
        zg = E(nc.sbuf_tensor("zg", [8, 1], F32))
        prb = E(nc.sbuf_tensor("prb", [8, B * TW], F32))
        ptok = E(nc.sbuf_tensor("ptok", [P, 8], F32))
        t2 = alloc_top2_scratch(nc, E)
        msk = E(nc.sbuf_tensor("msk", [P, 8], F32))
        wtok = E(nc.sbuf_tensor("wtok", [P, NTILES, 8], BF16))
        indt = E(nc.sbuf_tensor("indt", [P, NTILES, 8], F32))
        wstg = E(nc.sbuf_tensor("wstg", [P, 2, D], F32))
        wub = E(nc.sbuf_tensor("wub", [P, 2, 8, D], BF16))
        b1t = E(nc.sbuf_tensor("b1t", [P, 8, 8], F32))
        b2t = E(nc.sbuf_tensor("b2t", [P, 8, 8], F32))
        b3t = E(nc.sbuf_tensor("b3t", [1, 8, D], BF16))
        ones1 = E(nc.sbuf_tensor("ones1", [1, P], BF16))
        h1 = E(nc.sbuf_tensor("h1", [P, 8, B * TW], BF16))
        h2 = E(nc.sbuf_tensor("h2", [P, 8, B * TW], BF16))
        ysc = E(nc.sbuf_tensor("ysc", [P, D], BF16))
        vrow = E(nc.sbuf_tensor("vrow", [1, D], F32))

        pmm = PsumPool(p, ctx, "pmm", [P, 512], F32, 4)
        ppr = PsumPool(p, ctx, "ppr", [P, 8], F32, 2)
        pv = [E(nc.psum_tensor(f"pv{i}", [1, 512], F32)) for i in range(2)]

        def ldw(dst_ap, src_ap, wait=(), sem="ins"):
            p.do("sync", lambda e: e.dma_start(out=dst_ap, in_=src_ap),
                 wait=wait, inc=(sem, 16))

        ldw(xn2T[:], xn2Ti.rearrange("c p t -> p c t"))
        ldw(lgt[:], logiti[:])
        ldw(ast[:], astat[:])
        ldw(i8t[:], i8[:])
        ldw(b1t[:], b1c[:])
        ldw(b2t[:], b2c[:])
        ins_all = p.cnt["ins"]
        for eng in ("vector", "scalar", "tensor", "gpsimd"):
            p.wait(eng, "ins", ins_all)
        stg_seq = [0]

        def stage(load_fn, cast_fn):
            kk = stg_seq[0]
            sl = kk % 2
            sem = f"wst{sl}"
            p.do("sync", lambda e: load_fn(e, sl),
                 wait=[("wc", max(0, kk - 1))], inc=(sem, 16))
            p.do("gpsimd", lambda e: cast_fn(e, sl),
                 wait=[(sem, p.cnt[sem])], inc=("wc", 1))
            stg_seq[0] += 1

        for ex in range(8):
            stage(lambda e, sl, ex=ex: e.dma_start(out=wstg[0:1, sl, 0:D],
                                                   in_=b3r[ex:ex + 1, :]),
                  lambda e, sl, ex=ex: e.tensor_copy(b3t[0:1, ex, :], wstg[0:1, sl, 0:D]))
        p.do("vector", lambda e: e.memset(t2["z8"][:], 0.0))
        p.do("vector", lambda e: e.memset(ones1[:], 1.0), inc=("vp", 1))

        for b in range(B):
            mv = ast[:, 2 * b, :]
            zv = ast[:, 2 * b + 1, :]
            p.do("vector", lambda e, mv=mv: e.tensor_reduce(mg[:], mv, AX.X, ALU.max))
            p.do("vector", lambda e, b=b: e.tensor_single_scalar(
                Mneg[:, b:b + 1], mg[:], -1.0, ALU.mult))
            p.do("vector", lambda e, mv=mv, b=b: e.tensor_scalar(
                dsc[:], mv, Mneg[:, b:b + 1], None, ALU.add), inc=("vp", 1))
            p.do("scalar", lambda e: e.activation(esc[:], dsc[:], AF.Exp),
                 wait=[("vp", p.cnt["vp"])], inc=("sp", 1))
            p.do("vector", lambda e, zv=zv: e.tensor_tensor(esc[:], esc[:], zv, ALU.mult),
                 wait=[("sp", p.cnt["sp"])])
            p.do("vector", lambda e: e.tensor_reduce(zg[:], esc[:], AX.X, ALU.add))
            p.do("vector", lambda e, b=b: e.reciprocal(Zr[:, b:b + 1], zg[:]), inc=("vp", 1))
            p.do("scalar", lambda e, b=b: e.activation(
                prb[:, b * TW:(b + 1) * TW], lgt[:, b * TW:(b + 1) * TW], AF.Exp,
                bias=Mneg[:, b:b + 1]), wait=[("vp", p.cnt["vp"])], inc=("sp", 1))
            p.do("vector", lambda e, b=b: e.tensor_scalar(
                prb[:, b * TW:(b + 1) * TW], prb[:, b * TW:(b + 1) * TW],
                Zr[:, b:b + 1], None, ALU.mult),
                wait=[("sp", p.cnt["sp"])], inc=("prb", 1))
        prb_done = p.cnt["prb"]

        for ti in range(NTILES):
            c0 = ti * P
            mms = [lambda e, tile, c0=c0: e.matmul(tile[:], prb[:, c0:c0 + P], i8t[:],
                                                   start=True, stop=True)]

            def ev(e, tile):
                return e.tensor_copy(ptok[:], tile[:])
            ppr.group(mms, ev, pre_wait=[("prb", prb_done), ("top", p.cnt.get("top", 0))])
            emit_top2(p, ptok, t2, msk)
            p.do("vector", lambda e, ti=ti: e.tensor_tensor(wtok[:, ti, :], ptok[:], msk[:],
                                                            ALU.mult))
            p.do("vector", lambda e, ti=ti: e.tensor_copy(indt[:, ti, :], msk[:]),
                 inc=("top", 1))
        top_done = p.cnt["top"]
        p.do("gpsimd", lambda e: e.dma_start(
            out=indo.rearrange("(n p) e -> p n e", p=P),
            in_=indt[:]),
            wait=[("top", top_done)], inc=("iout", 16), kind="dma")

        stage_done = {}
        sg = 0
        for ex in range(8):
            wready = {}
            for si, wsrc in enumerate((w1, w2, w3)):
                wb = sg % 2
                for ic in range(8):
                    def lf(e, sl, wsrc=wsrc, ex=ex, ic=ic):
                        return e.dma_start(out=wstg[:, sl, :],
                                           in_=wsrc[ex, ic * P:(ic + 1) * P, :])

                    def cf(e, sl, wb=wb, ic=ic):
                        return e.tensor_copy(wub[:, wb, ic, :], wstg[:, sl, :])
                    kk = stg_seq[0]
                    sl = kk % 2
                    sem = f"wst{sl}"
                    p.do("sync", lambda e, lf=lf, sl=sl: lf(e, sl),
                         wait=[("wc", max(0, kk - 1)),
                               ("pmm_ev", stage_done.get(sg - 2, 0))],
                         inc=(sem, 16))
                    p.do("gpsimd", lambda e, cf=cf, sl=sl: cf(e, sl),
                         wait=[(sem, p.cnt[sem])], inc=("wc", 1))
                    stg_seq[0] += 1
                wready[si] = p.cnt["wc"]
                if si < 2:
                    src, dst, bt = (xn2T, h1, b1t) if si == 0 else (h1, h2, b2t)
                    done_at = p.cnt["pmm_ev"]
                    for oc in range(8):
                        for g in range(4):
                            c0 = g * 512
                            mms = []
                            for ic in range(8):
                                mms.append(lambda e, tile, ic=ic, oc=oc, c0=c0, src=src,
                                           wb=wb: e.matmul(
                                               tile[:], wub[:, wb, ic, oc * P:(oc + 1) * P],
                                               src[:, ic, c0:c0 + 512],
                                               start=(ic == 0), stop=(ic == 7)))

                            def ev(e, tile, ex=ex, oc=oc, c0=c0, dst=dst, bt=bt):
                                return e.activation(dst[:, oc, c0:c0 + 512], tile[:],
                                                    AF.Relu, bias=bt[:, ex, oc:oc + 1])
                            pre = []
                            if oc == 0 and g == 0:
                                pre = [("wc", wready[si]), ("pmm_ev", done_at)]
                            pmm.group(mms, ev, evac_eng="scalar", pre_wait=pre)
                    stage_done[sg] = p.cnt["pmm_ev"]
                    sg += 1
            h2_done = p.cnt["pmm_ev"]
            wb = sg % 2
            for ti in range(NTILES):
                tc = ti * P
                for h in range(2):
                    mms = []
                    for ic in range(8):
                        mms.append(lambda e, tile, ic=ic, tc=tc, h=h, wb=wb: e.matmul(
                            tile[:], h2[:, ic, tc:tc + P],
                            wub[:, wb, ic, h * 512:(h + 1) * 512],
                            start=(ic == 0), stop=False))
                    mms.append(lambda e, tile, ex=ex, h=h: e.matmul(
                        tile[:], ones1[:], b3t[0:1, ex, h * 512:(h + 1) * 512],
                        start=False, stop=True))

                    def ev(e, tile, h=h):
                        return e.tensor_single_scalar(ysc[:, h * 512:(h + 1) * 512],
                                                      tile[:], 0.0, ALU.max)
                    pre = [("wc", wready[2]), ("pmm_ev", h2_done),
                           ("top", top_done)] if (ti == 0 and h == 0) else ()
                    pmm.group(mms, ev, pre_wait=pre)
                yv = p.cnt["pmm_ev"]
                for h in range(2):
                    p.do("tensor", lambda e, ex=ex, ti=ti, h=h: e.matmul(
                        pv[h][:], wtok[:, ti, ex:ex + 1], ysc[:, h * 512:(h + 1) * 512],
                        start=(ti == 0), stop=(ti == NTILES - 1), skip_group_check=True),
                        wait=[("pmm_ev", yv)] if h == 0 else (),
                        inc=("vmm", 1) if (ti == NTILES - 1 and h == 1) else None)
            stage_done[sg] = p.cnt["pmm_ev"]
            sg += 1
            p.do("vector", lambda e, ex=ex: e.tensor_copy(vrow[0:1, 0:512], pv[0][:]),
                 wait=[("vmm", p.cnt["vmm"]), ("vout", ex * 16)])
            p.do("vector", lambda e: e.tensor_copy(vrow[0:1, 512:1024], pv[1][:]),
                 inc=("vrow", 1))
            p.do("gpsimd", lambda e, ex=ex: e.dma_start(out=vparto[ex:ex + 1, :],
                                                        in_=vrow[0:1, :]),
                 wait=[("vrow", p.cnt["vrow"])], inc=("vout", 16), kind="dma")
            p.wait("tensor", "vrow", p.cnt["vrow"])

        p.wait("gpsimd", "vout", p.cnt["vout"])
        p.wait("gpsimd", "iout", p.cnt["iout"])
        p.finalize()
    return nc


def build_L4():
    nc = bass.Bass("TRN2", target_bir_lowering=False, debug=False)
    dt = nc.dram_tensor
    x1i = dt("x1i", [B, TW, D], F32, kind="ExternalInput").ap()
    indi = dt("indi", [B * TW, 8], F32, kind="ExternalInput").ap()
    vparts = dt("vparts", [NC, 8, D], F32, kind="ExternalInput").ap()
    wqT = dt("wqT", [D, D], F32, kind="ExternalInput").ap()
    wkT = dt("wkT", [D, D], F32, kind="ExternalInput").ap()
    wvT = dt("wvT", [D, D], F32, kind="ExternalInput").ap()
    bqb = dt("bqb", [P, D], F32, kind="ExternalInput").ap()
    bkb = dt("bkb", [P, D], F32, kind="ExternalInput").ap()
    bvb = dt("bvb", [P, D], F32, kind="ExternalInput").ap()
    cosb = dt("cosb", [P, D], F32, kind="ExternalInput").ap()
    sinb = dt("sinb", [P, D], F32, kind="ExternalInput").ap()
    identb = dt("identb", [P, P], BF16, kind="ExternalInput").ap()

    x3o = dt("x3o", [B, TW, D], F32, kind="ExternalOutput").ap()

    with ExitStack() as ctx:
        E = ctx.enter_context
        p = Prog(nc, ctx)

        vstg = E(nc.sbuf_tensor("vstg", [8, 2, D], F32))
        vacc = E(nc.sbuf_tensor("vacc", [8, D], F32))
        vb = E(nc.sbuf_tensor("vb", [8, D], BF16))
        idb = E(nc.sbuf_tensor("idb", [P, P], BF16))
        wstg = E(nc.sbuf_tensor("wstg", [P, 1, D], F32))
        wqb = E(nc.sbuf_tensor("wqb", [P, 8, D], BF16))
        wkb = E(nc.sbuf_tensor("wkb", [P, 8, D], BF16))
        wvb = E(nc.sbuf_tensor("wvb", [P, 8, D], BF16))
        bqt = E(nc.sbuf_tensor("bqt", [P, D], F32))
        bkt = E(nc.sbuf_tensor("bkt", [P, D], F32))
        bvt = E(nc.sbuf_tensor("bvt", [P, D], F32))
        cost = E(nc.sbuf_tensor("cost", [P, D], F32))
        sint = E(nc.sbuf_tensor("sint", [P, D], F32))
        x2t = E(nc.sbuf_tensor("x2t", [P, NTILES, D], F32))
        x1tile = E(nc.sbuf_tensor("x1tile", [P, 2, D], F32))
        indtile = E(nc.sbuf_tensor("indtile", [P, 2, 8], F32))
        indbf = E(nc.sbuf_tensor("indbf", [P, 8], BF16))
        indT = E(nc.sbuf_tensor("indT", [8, P], BF16))
        xnb = E(nc.sbuf_tensor("xnb", [P, D], BF16))
        xnT = E(nc.sbuf_tensor("xnT", [P, 8, 2 * P], BF16))
        scr = alloc_ln_scratch(nc, E, p)
        q1 = E(nc.sbuf_tensor("q1", [P, D], F32))
        k0 = E(nc.sbuf_tensor("k0", [P, D], F32))
        k1 = E(nc.sbuf_tensor("k1", [P, D], F32))
        v0 = E(nc.sbuf_tensor("v0", [P, D], F32))
        v1 = E(nc.sbuf_tensor("v1", [P, D], F32))
        rtmp = E(nc.sbuf_tensor("rtmp", [P, D], F32))
        q1r = E(nc.sbuf_tensor("q1r", [P, D], F32))
        k1r = E(nc.sbuf_tensor("k1r", [P, D], F32))
        mtmp = E(nc.sbuf_tensor("mtmp", [P, D], F32))
        s10 = E(nc.sbuf_tensor("s10", [P, 16], F32))
        s11 = E(nc.sbuf_tensor("s11", [P, 16], F32))
        mx = E(nc.sbuf_tensor("mx", [P, 16], F32))
        e10 = E(nc.sbuf_tensor("e10", [P, 16], F32))
        e11 = E(nc.sbuf_tensor("e11", [P, 16], F32))
        den = E(nc.sbuf_tensor("den", [P, 16], F32))
        a0 = E(nc.sbuf_tensor("a0", [P, 16], F32))
        a1 = E(nc.sbuf_tensor("a1", [P, 16], F32))
        o1 = rtmp

        pmm = PsumPool(p, ctx, "pmm", [P, 512], F32, 4)
        pit = PsumPool(p, ctx, "pit", [8, P], F32, 2)
        ptr = [E(nc.psum_tensor(f"ptr{i}", [P, P], BF16)) for i in range(2)]

        def ldw(dst_ap, src_ap, wait=(), sem="ins"):
            p.do("sync", lambda e: e.dma_start(out=dst_ap, in_=src_ap),
                 wait=wait, inc=(sem, 16))

        ldw(idb[:], identb[:])
        ldw(bqt[:], bqb[:])
        ldw(bkt[:], bkb[:])
        ldw(bvt[:], bvb[:])
        ldw(cost[:], cosb[:])
        ldw(sint[:], sinb[:])
        ins_all = p.cnt["ins"]
        for eng in ("vector", "scalar", "tensor", "gpsimd"):
            p.wait(eng, "ins", ins_all)
        for c in range(NC):
            ldw(vstg[:, c % 2, :], vparts[c],
                wait=[("vacc", c)], sem="vls")
            if c == 0:
                p.do("vector", lambda e: e.tensor_copy(vacc[:], vstg[:, 0, :]),
                     wait=[("vls", p.cnt["vls"])], inc=("vacc", 1))
            else:
                p.do("vector", lambda e, c=c: e.tensor_tensor(
                    vacc[:], vacc[:], vstg[:, c % 2, :], ALU.add),
                    wait=[("vls", p.cnt["vls"])], inc=("vacc", 1))
        p.do("vector", lambda e: e.tensor_copy(vb[:], vacc[:]), inc=("vacc", 1))
        vacc_done = p.cnt["vacc"]
        kk = 0
        for ic in range(8):
            for wsrc, wdst in ((wqT, wqb), (wkT, wkb), (wvT, wvb)):
                ldw(wstg[:, 0, :], wsrc[ic * P:(ic + 1) * P, :],
                    wait=[("wc", kk)], sem="wst")
                p.do("gpsimd", lambda e, wdst=wdst, ic=ic: e.tensor_copy(
                    wdst[:, ic, :], wstg[:, 0, :]),
                    wait=[("wst", p.cnt["wst"])], inc=("wc", 1))
                kk += 1
        w_all = p.cnt["wc"]

        tr_state = [0]
        x2ev = {}
        for j in range(NT):
            for b in range(B):
                ti = b * NT + j
                k = 2 * j + b
                buf = k % 2
                gate = x2ev.get(k - 1, 0)
                p.do("sync", lambda e, b=b, j=j, buf=buf: e.dma_start(
                    out=x1tile[:, buf, :], in_=x1i[b, j * P:(j + 1) * P, :]),
                    wait=[("pmm_ev", gate)], inc=("xtl", 16))
                p.do("sync", lambda e, b=b, j=j, buf=buf: e.dma_start(
                    out=indtile[:, buf, :],
                    in_=indi[(b * TW + j * P):(b * TW + (j + 1) * P), :]), inc=("xtl", 16))
                in_now = p.cnt["xtl"]
                p.do("vector", lambda e, buf=buf: e.tensor_copy(indbf[:], indtile[:, buf, :]),
                     wait=[("xtl", in_now), ("pit_ev", p.cnt["pit_ev"])],
                     inc=("vp", 1))
                mms = [lambda e, tile: e.matmul(tile[:], indbf[:], idb[:],
                                                start=True, stop=True)]

                def evT(e, tile):
                    return e.tensor_copy(indT[:], tile[:])
                pit.group(mms, evT, pre_wait=[("vp", p.cnt["vp"])])
                itr = p.cnt["pit_ev"]
                for h in range(2):
                    mms = [lambda e, tile, h=h: e.matmul(
                        tile[:], indT[:], vb[:, h * 512:(h + 1) * 512],
                        start=True, stop=True)]

                    def ev(e, tile, ti=ti, buf=buf, h=h):
                        return e.tensor_tensor(x2t[:, ti, h * 512:(h + 1) * 512], tile[:],
                                               x1tile[:, buf, h * 512:(h + 1) * 512], ALU.add)
                    pmm.group(mms, ev, pre_wait=[("pit_ev", itr), ("vacc", vacc_done),
                                                 ("wc", w_all)])
                x2ev[k] = p.cnt["pmm_ev"]

                def consume(p_, mneg, rstd, ti=ti):
                    p_.do("vector", lambda e: e.tensor_scalar(
                        xnb[:], x2t[:, ti, :], mneg[:], rstd[:], ALU.add, ALU.mult),
                        wait=[("tp", p_.cnt["tp"])], inc=("vp", 1))

                ln_emit(p, x2t[:, ti, :], consume, scr,
                        extra=[("pmm_ev", x2ev[k])])
                emit_transposes(p, xnb, xnT, (b * P, (b + 1) * P), ptr, idb, tr_state)
            pair_T = p.cnt["tp_ev"]
            for dst, wgt, bias, bcol in ((q1, wqb, bqt, 1), (k0, wkb, bkt, 0),
                                         (k1, wkb, bkt, 1), (v0, wvb, bvt, 0),
                                         (v1, wvb, bvt, 1)):
                for h in range(2):
                    mms = []
                    for ic in range(8):
                        mms.append(lambda e, tile, ic=ic, bcol=bcol, wgt=wgt, h=h:
                                   e.matmul(tile[:], xnT[:, ic, bcol * P:(bcol + 1) * P],
                                            wgt[:, ic, h * 512:(h + 1) * 512],
                                            start=(ic == 0), stop=(ic == 7)))

                    def ev(e, tile, dst=dst, bias=bias, h=h):
                        return e.tensor_tensor(dst[:, h * 512:(h + 1) * 512], tile[:],
                                               bias[:, h * 512:(h + 1) * 512], ALU.add)
                    pmm.group(mms, ev, pre_wait=[("tp_ev", pair_T)])
            qkv_done = p.cnt["pmm_ev"]
            for src, dstp in ((q1, q1r), (k1, k1r)):
                p.do("vector", lambda e, src=src: e.tensor_scalar(
                    rtmp[:].rearrange("p (h k) -> p h k", k=64)[:, :, 0:32],
                    src[:].rearrange("p (h k) -> p h k", k=64)[:, :, 32:64],
                    -1.0, None, ALU.mult),
                    wait=[("pmm_ev", qkv_done)] if src is q1 else ())
                p.do("vector", lambda e, src=src: e.tensor_copy(
                    rtmp[:].rearrange("p (h k) -> p h k", k=64)[:, :, 32:64],
                    src[:].rearrange("p (h k) -> p h k", k=64)[:, :, 0:32]))
                p.do("vector", lambda e, src=src, dstp=dstp: e.tensor_tensor(
                    dstp[:], src[:], cost[:], ALU.mult))
                p.do("vector", lambda e: e.tensor_tensor(mtmp[:], rtmp[:], sint[:], ALU.mult))
                p.do("vector", lambda e, dstp=dstp: e.tensor_tensor(
                    dstp[:], dstp[:], mtmp[:], ALU.add), inc=("vp", 1))
            p.do("vector", lambda e: e.tensor_tensor(mtmp[:], q1r[:], k0[:], ALU.mult))
            p.do("vector", lambda e: e.tensor_reduce(
                s10[:], mtmp[:].rearrange("p (h k) -> p h k", k=64), AX.X, ALU.add))
            p.do("vector", lambda e: e.tensor_single_scalar(s10[:], s10[:], 0.125, ALU.mult))
            p.do("vector", lambda e: e.tensor_tensor(mtmp[:], q1r[:], k1r[:], ALU.mult))
            p.do("vector", lambda e: e.tensor_reduce(
                s11[:], mtmp[:].rearrange("p (h k) -> p h k", k=64), AX.X, ALU.add))
            p.do("vector", lambda e: e.tensor_single_scalar(s11[:], s11[:], 0.125, ALU.mult))
            p.do("vector", lambda e: e.tensor_tensor(mx[:], s10[:], s11[:], ALU.max))
            p.do("vector", lambda e: e.tensor_tensor(s10[:], s10[:], mx[:], ALU.subtract))
            p.do("vector", lambda e: e.tensor_tensor(s11[:], s11[:], mx[:], ALU.subtract),
                 inc=("vp", 1))
            p.do("scalar", lambda e: e.activation(e10[:], s10[:], AF.Exp),
                 wait=[("vp", p.cnt["vp"])])
            p.do("scalar", lambda e: e.activation(e11[:], s11[:], AF.Exp), inc=("sp", 1))
            p.do("vector", lambda e: e.tensor_tensor(den[:], e10[:], e11[:], ALU.add),
                 wait=[("sp", p.cnt["sp"])])
            p.do("vector", lambda e: e.tensor_single_scalar(den[:], den[:], 1.0, ALU.add))
            p.do("vector", lambda e: e.reciprocal(den[:], den[:]))
            p.do("vector", lambda e: e.tensor_tensor(a0[:], e10[:], den[:], ALU.mult))
            p.do("vector", lambda e: e.tensor_tensor(a1[:], e11[:], den[:], ALU.mult))
            p.do("vector", lambda e: e.tensor_tensor(
                o1[:].rearrange("p (h k) -> p h k", k=64),
                a0[:].rearrange("p (h one) -> p h one", one=1).broadcast_to([P, 16, 64]),
                v0[:].rearrange("p (h k) -> p h k", k=64), ALU.mult))
            p.do("vector", lambda e: e.tensor_tensor(
                mtmp[:].rearrange("p (h k) -> p h k", k=64),
                a1[:].rearrange("p (h one) -> p h one", one=1).broadcast_to([P, 16, 64]),
                v1[:].rearrange("p (h k) -> p h k", k=64), ALU.mult))
            p.do("vector", lambda e: e.tensor_tensor(o1[:], o1[:], mtmp[:], ALU.add))
            ti0, ti1 = j, NT + j
            p.do("vector", lambda e, ti1=ti1: e.tensor_tensor(
                x2t[:, ti1, :], x2t[:, ti1, :], o1[:], ALU.add))
            p.do("vector", lambda e, ti0=ti0: e.scalar_tensor_tensor(
                x2t[:, ti0, :], v0[:], 0.5, x2t[:, ti0, :], ALU.mult, ALU.add),
                inc=("x3", 1))
            x3n = p.cnt["x3"]
            p.do("gpsimd", lambda e, j=j: e.dma_start(
                out=x3o[0, j * P:(j + 1) * P, :], in_=x2t[:, j, :]),
                wait=[("x3", x3n)], inc=("out", 16), kind="dma")
            p.do("gpsimd", lambda e, j=j: e.dma_start(
                out=x3o[1, j * P:(j + 1) * P, :], in_=x2t[:, NT + j, :]),
                wait=[("x3", x3n)], inc=("out", 16), kind="dma")

        p.wait("gpsimd", "out", p.cnt["out"])
        p.finalize()
    return nc


def build_L5():
    nc = bass.Bass("TRN2", target_bir_lowering=False, debug=False)
    dt = nc.dram_tensor
    x3i = dt("x3i", [B, TW, D], F32, kind="ExternalInput").ap()
    wg = dt("wg", [D, 8], F32, kind="ExternalInput").ap()
    bgb = dt("bgb", [P, 8], F32, kind="ExternalInput").ap()
    w1 = dt("w1", [8, D, 4096], F32, kind="ExternalInput").ap()
    b1c = dt("b1c", [P, 8, 32], F32, kind="ExternalInput").ap()
    w2 = dt("w2", [8, 4096, D], F32, kind="ExternalInput").ap()
    b2r = dt("b2r", [8, D], F32, kind="ExternalInput").ap()
    identb = dt("identb", [P, P], BF16, kind="ExternalInput").ap()

    outo = dt("outo", [B, TW, D], F32, kind="ExternalOutput").ap()

    TG = 512
    NG = (B * TW) // TG

    with ExitStack() as ctx:
        E = ctx.enter_context
        p = Prog(nc, ctx)

        x3t = E(nc.sbuf_tensor("x3t", [P, NTILES, D], F32))
        xnb = E(nc.sbuf_tensor("xnb", [P, D], BF16))
        xnT = E(nc.sbuf_tensor("xnT", [P, 8, B * TW], BF16))
        idb = E(nc.sbuf_tensor("idb", [P, P], BF16))
        scr = alloc_ln_scratch(nc, E, p)
        wgb = E(nc.sbuf_tensor("wgb", [P, 8, 8], BF16))
        wstg = E(nc.sbuf_tensor("wstg", [P, 2, 1024], F32))
        bgt = E(nc.sbuf_tensor("bgt", [P, 8], F32))
        lgt = E(nc.sbuf_tensor("lgt", [P, 8], F32))
        mcol = E(nc.sbuf_tensor("mcol", [P, 1], F32))
        esum = E(nc.sbuf_tensor("esum", [P, 1], F32))
        gts = E(nc.sbuf_tensor("gts", [P, 8], F32))
        t2 = alloc_top2_scratch(nc, E)
        msk = E(nc.sbuf_tensor("msk", [P, 8], F32))
        gsel = E(nc.sbuf_tensor("gsel", [P, 8], F32))
        gn = E(nc.sbuf_tensor("gn", [P, NTILES, 8], F32))
        b1t = E(nc.sbuf_tensor("b1t", [P, 8, 32], F32))
        b2t = E(nc.sbuf_tensor("b2t", [1, 8, D], BF16))
        ones1 = E(nc.sbuf_tensor("ones1", [1, P], BF16))
        w1b = E(nc.sbuf_tensor("w1b", [P, 2, 8, P], BF16))
        w2b = E(nc.sbuf_tensor("w2b", [P, 32, 512], BF16))
        h = E(nc.sbuf_tensor("h", [P, 32, TG], BF16))

        pmm = PsumPool(p, ctx, "pmm", [P, 512], F32, 4)
        plog = PsumPool(p, ctx, "plog", [P, 8], F32, 2)
        ptr = [E(nc.psum_tensor(f"ptr{i}", [P, P], BF16)) for i in range(2)]

        def ldw(dst_ap, src_ap, wait=(), sem="ins"):
            p.do("sync", lambda e: e.dma_start(out=dst_ap, in_=src_ap),
                 wait=wait, inc=(sem, 16))

        ldw(idb[:], identb[:])
        ldw(bgt[:], bgb[:])
        ldw(b1t[:], b1c[:])
        ins_all = p.cnt["ins"]
        for eng in ("vector", "scalar", "tensor", "gpsimd"):
            p.wait(eng, "ins", ins_all)
        stg_seq = [0]

        def stage(load_fn, cast_fn, extra_load_wait=()):
            kk = stg_seq[0]
            sl = kk % 2
            sem = f"wst{sl}"
            p.do("sync", lambda e: load_fn(e, sl),
                 wait=[("wc", max(0, kk - 1))] + list(extra_load_wait), inc=(sem, 16))
            p.do("gpsimd", lambda e: cast_fn(e, sl),
                 wait=[(sem, p.cnt[sem])], inc=("wc", 1))
            stg_seq[0] += 1

        for ex in range(8):
            stage(lambda e, sl, ex=ex: e.dma_start(out=wstg[0:1, sl, 0:D],
                                                   in_=b2r[ex:ex + 1, :]),
                  lambda e, sl, ex=ex: e.tensor_copy(b2t[0:1, ex, :],
                                                     wstg[0:1, sl, 0:D]))
        p.do("vector", lambda e: e.memset(t2["z8"][:], 0.0))
        p.do("vector", lambda e: e.memset(ones1[:], 1.0), inc=("vp", 1))
        for ic in range(8):
            stage(lambda e, sl, ic=ic: e.dma_start(out=wstg[:, sl, 0:8],
                                                   in_=wg[ic * P:(ic + 1) * P, :]),
                  lambda e, sl, ic=ic: e.tensor_copy(wgb[:, ic, :], wstg[:, sl, 0:8]))
        wg_all = p.cnt["wc"]

        tr_state = [0]
        for b in range(B):
            for j in range(NT):
                ti = b * NT + j
                p.do("sync", lambda e, b=b, j=j, ti=ti: e.dma_start(
                    out=x3t[:, ti, :], in_=x3i[b, j * P:(j + 1) * P, :]),
                    wait=[("wc", wg_all)] if ti == 0 else (), inc=("x3l", 16))
                in_now = 16 * NTILES

                def consume(p_, mneg, rstd, ti=ti):
                    p_.do("vector", lambda e: e.tensor_scalar(
                        xnb[:], x3t[:, ti, :], mneg[:], rstd[:], ALU.add, ALU.mult),
                        wait=[("tp", p_.cnt["tp"])], inc=("vp", 1))

                ln_emit(p, x3t[:, ti, :], consume, scr, extra=[("x3l", in_now)])
                emit_transposes(p, xnb, xnT, (ti * P, (ti + 1) * P), ptr, idb, tr_state)
                tdone = p.cnt["tp_ev"]
                mms = []
                for ic in range(8):
                    mms.append(lambda e, tile, ic=ic, ti=ti: e.matmul(
                        tile[:], xnT[:, ic, ti * P:(ti + 1) * P], wgb[:, ic, :],
                        start=(ic == 0), stop=(ic == 7)))

                def ev(e, tile):
                    return e.tensor_tensor(lgt[:], tile[:], bgt[:], ALU.add)
                plog.group(mms, ev, pre_wait=[("tp_ev", tdone), ("gate", p.cnt.get("gate", 0))])
                gv = p.cnt["plog_ev"]
                p.do("vector", lambda e: e.tensor_reduce(mcol[:], lgt[:], AX.X, ALU.max),
                     wait=[("plog_ev", gv)])
                p.do("vector", lambda e: e.tensor_single_scalar(mcol[:], mcol[:], -1.0,
                                                                ALU.mult), inc=("vp", 1))
                p.do("scalar", lambda e: e.activation(gts[:], lgt[:], AF.Exp, bias=mcol[:],
                                                      accum_out=esum[:]),
                     wait=[("vp", p.cnt["vp"])], inc=("sp", 1))
                p.do("vector", lambda e: e.reciprocal(esum[:], esum[:]),
                     wait=[("sp", p.cnt["sp"])])
                p.do("vector", lambda e: e.tensor_scalar(gts[:], gts[:], esum[:], None,
                                                         ALU.mult))
                emit_top2(p, gts, t2, msk)
                p.do("vector", lambda e: e.tensor_tensor(gsel[:], gts[:], msk[:], ALU.mult))
                p.do("vector", lambda e: e.tensor_reduce(mcol[:], gsel[:], AX.X, ALU.add))
                p.do("vector", lambda e: e.tensor_single_scalar(mcol[:], mcol[:], 1e-12,
                                                                ALU.max))
                p.do("vector", lambda e: e.reciprocal(mcol[:], mcol[:]))
                p.do("vector", lambda e, ti=ti: e.tensor_scalar(
                    gn[:, ti, :], gsel[:], mcol[:], None, ALU.mult), inc=("gate", 1))
        gates_done = p.cnt["gate"]
        xnT_done = p.cnt["tp_ev"]

        w1_hist = []
        for ex in range(8):
            for tg in range(NG):
                tc0 = tg * TG
                for hc in range(32):
                    wb = stg_seq[0] % 2
                    w1gate = w1_hist[-2] if len(w1_hist) >= 2 else 0
                    stage(lambda e, sl, ex=ex, hc=hc: e.dma_start(
                        out=wstg[:, sl, :].rearrange("p (a b) -> p a b", b=P),
                        in_=w1[ex][:, hc * P:(hc + 1) * P].rearrange(
                            "(ic p) o -> p ic o", p=P)),
                        lambda e, sl, wb=wb: e.tensor_copy(
                            w1b[:, wb % 2, :, :].rearrange("p a b -> p (a b)"),
                            wstg[:, sl, :]),
                        extra_load_wait=[("pmm_ev", w1gate)])
                    wnow = p.cnt["wc"]
                    for g in range(TG // 512):
                        c0 = tc0 + g * 512
                        mms = []
                        for ic in range(8):
                            mms.append(lambda e, tile, ic=ic, wb=wb, c0=c0: e.matmul(
                                tile[:], w1b[:, wb, ic, :], xnT[:, ic, c0:c0 + 512],
                                start=(ic == 0), stop=(ic == 7)))

                        def ev(e, tile, ex=ex, hc=hc, gg=g):
                            return e.activation(h[:, hc, gg * 512:(gg + 1) * 512], tile[:],
                                                AF.Gelu, bias=b1t[:, ex, hc:hc + 1])
                        pre = [("wc", wnow)]
                        if ex == 0 and tg == 0 and hc == 0 and g == 0:
                            pre.append(("tp_ev", xnT_done))
                        pmm.group(mms, ev, evac_eng="scalar", pre_wait=pre)
                    w1_hist.append(p.cnt["pmm_ev"])
                h_done = p.cnt["pmm_ev"]
                for ob in range(2):
                    sweep_gate = p.cnt["pmm_ev"]
                    for hc in range(32):
                        stage(lambda e, sl, ex=ex, hc=hc, ob=ob: e.dma_start(
                            out=wstg[:, sl, 0:512],
                            in_=w2[ex, hc * P:(hc + 1) * P, ob * 512:(ob + 1) * 512]),
                            lambda e, sl, hc=hc: e.tensor_copy(
                                w2b[:, hc, :], wstg[:, sl, 0:512]),
                            extra_load_wait=[("pmm_ev", sweep_gate)])
                    w2now = p.cnt["wc"]
                    for tt in range(TG // P):
                        ti = (tc0 // P) + tt
                        mms = []
                        for hc in range(32):
                            mms.append(lambda e, tile, hc=hc, tt=tt, tc0=tc0: e.matmul(
                                tile[:], h[:, hc, tt * P:(tt + 1) * P], w2b[:, hc, :],
                                start=(hc == 0), stop=False))
                        mms.append(lambda e, tile, ex=ex, ob=ob: e.matmul(
                            tile[:], ones1[:], b2t[0:1, ex, ob * 512:(ob + 1) * 512],
                            start=False, stop=True))

                        def ev(e, tile, ti=ti, ob=ob, ex=ex):
                            return e.scalar_tensor_tensor(
                                x3t[:, ti, ob * 512:(ob + 1) * 512], tile[:],
                                gn[:, ti, ex:ex + 1], x3t[:, ti, ob * 512:(ob + 1) * 512],
                                ALU.mult, ALU.add)
                        pre = [("wc", w2now), ("pmm_ev", h_done)]
                        if ex == 0 and tg == 0 and ob == 0 and tt == 0:
                            pre.append(("gate", gates_done))
                        pmm.group(mms, ev, pre_wait=pre)

        p.wait("gpsimd", "pmm_ev", p.cnt["pmm_ev"])
        for b in range(B):
            for j in range(NT):
                ti = b * NT + j
                p.do("gpsimd", lambda e, b=b, j=j, ti=ti: e.dma_start(
                    out=outo[b, j * P:(j + 1) * P, :], in_=x3t[:, ti, :]), inc=("out", 16), kind="dma")
        p.wait("gpsimd", "out", p.cnt["out"])
        p.finalize()
    return nc


def _prep(inp):
    f32 = np.float32
    c = lambda a: np.ascontiguousarray(a, dtype=f32)
    pp = {}
    pp["identb"] = np.eye(P, dtype=ml_dtypes.bfloat16)
    pp["i8"] = np.eye(8, dtype=f32)

    pp["inwT"] = c(inp["L0_attn_in_w"].T)
    pp["inbc"] = c(np.asarray(inp["L0_attn_in_b"]).reshape(16, P).T)
    pp["nwb"] = c(np.broadcast_to(inp["L0_attn_norm_w"], (P, D)))
    pp["nbb"] = c(np.broadcast_to(inp["L0_attn_norm_b"], (P, D)))
    cw = np.asarray(inp["L0_attn_conv_w"])[:, 0, :]
    pp["convw"] = c(cw.reshape(8, P, 3).transpose(1, 0, 2))
    pp["convb"] = c(np.asarray(inp["L0_attn_conv_b"]).reshape(8, P).T)
    pp["dfm"] = c(np.asarray(inp["L0_attn_d"]).reshape(8, P).T)
    pp["dtwT"] = c(inp["L0_attn_dt_w"].T)
    pp["bwT"] = c(inp["L0_attn_b_w"].T)
    pp["cwT"] = c(inp["L0_attn_c_w"].T)
    pp["dtb"] = c(np.asarray(inp["L0_attn_dt_b"])[:, None])
    pp["bbias"] = c(np.asarray(inp["L0_attn_b_b"])[:, None])
    pp["cbias"] = c(np.asarray(inp["L0_attn_c_b"])[:, None])
    pp["alog"] = c(np.asarray(inp["L0_attn_a_log"])[:, None])
    ow = np.asarray(inp["L0_attn_out_w"])
    pp["outwT1"] = c(ow.T[:D, :])
    pp["outwT2"] = c(ow.T[D:, :])
    pp["outb"] = c(np.broadcast_to(inp["L0_attn_out_b"], (P, D)))

    pp["gatew"] = c(inp["L0_ffn_gate_w"])
    pp["gateb"] = c(np.asarray(inp["L0_ffn_gate_b"])[:, None])
    pp["w1"] = c(inp["L0_ffn_w1"])
    pp["w2"] = c(inp["L0_ffn_w2"])
    pp["w3"] = c(inp["L0_ffn_w3"])
    pp["b1c"] = c(np.asarray(inp["L0_ffn_b1"]).reshape(8, 8, P).transpose(2, 0, 1))
    pp["b2c"] = c(np.asarray(inp["L0_ffn_b2"]).reshape(8, 8, P).transpose(2, 0, 1))
    pp["b3r"] = c(inp["L0_ffn_b3"])

    pp["wqT"] = c(inp["L1_attn_Wq"].T)
    pp["wkT"] = c(inp["L1_attn_Wk"].T)
    pp["wvT"] = c(inp["L1_attn_Wv"].T)
    pp["bqb"] = c(np.broadcast_to(inp["L1_attn_bq"], (P, D)))
    pp["bkb"] = c(np.broadcast_to(inp["L1_attn_bk"], (P, D)))
    pp["bvb"] = c(np.broadcast_to(inp["L1_attn_bv"], (P, D)))
    ar = np.arange(0, 64, 2, dtype=np.float64)
    theta = 1.0 / (np.power(10000.0, ar) / 64.0)
    ang = np.concatenate([theta, theta])
    pp["cosb"] = c(np.broadcast_to(np.tile(np.cos(ang), 16).astype(f32), (P, D)))
    pp["sinb"] = c(np.broadcast_to(np.tile(np.sin(ang), 16).astype(f32), (P, D)))

    pp["wgf"] = c(inp["L1_ffn_Wg"])
    pp["bgb"] = c(np.broadcast_to(inp["L1_ffn_bg"], (P, 8)))
    pp["w1f"] = c(inp["L1_ffn_W1"])
    pp["b1f"] = c(np.asarray(inp["L1_ffn_b1"]).reshape(8, 32, P).transpose(2, 0, 1))
    pp["w2f"] = c(inp["L1_ffn_W2"])
    pp["b2f"] = c(inp["L1_ffn_b2"])
    return pp


def kernel(**inp):
    pp = _prep(inp)
    x = np.asarray(inp["L0_attn_x"], dtype=np.float32)
    cores = list(range(NC))

    nc1 = build_L1()
    maps1 = []
    for ci in cores:
        t0 = ci * TW
        xp = np.zeros((B, PADW, D), np.float32)
        lo, hi = max(0, t0 - P), min(T, t0 + TW + P)
        xp[:, (lo - (t0 - P)):(hi - (t0 - P)), :] = x[:, lo:hi, :]
        maps1.append(dict(
            xpad=xp,
            maskl=np.full((P, 1), 0.0 if ci == 0 else 1.0, np.float32),
            maskr=np.full((P, 1), 0.0 if ci == NC - 1 else 1.0, np.float32),
            identb=pp["identb"], nwb=pp["nwb"], nbb=pp["nbb"], inwT=pp["inwT"],
            inbc=pp["inbc"],
            convw=pp["convw"], convb=pp["convb"], dfm=pp["dfm"],
            dtwT=pp["dtwT"], bwT=pp["bwT"], cwT=pp["cwT"],
            dtb=pp["dtb"], bbias=pp["bbias"], cbias=pp["cbias"], alog=pp["alog"],
            outwT1=pp["outwT1"], outb=pp["outb"]))
    r1 = run_bass_kernel_spmd(nc1, maps1, core_ids=cores)
    dec = np.concatenate([r1.results[c]["deco"] for c in cores], axis=2)
    bbf = np.concatenate([r1.results[c]["bbo"] for c in cores], axis=2)

    nc2 = build_L2()
    maps2 = []
    for ci in cores:
        t0 = ci * TW
        decz = np.zeros((B, S, T), np.float32)
        bbz = np.zeros((B, S, T), np.float32)
        decz[:, :, T - (t0 + TW):] = dec[:, :, :t0 + TW]
        bbz[:, :, T - (t0 + TW):] = bbf[:, :, :t0 + TW]
        maps2.append(dict(
            residi=r1.results[ci]["resid"], projpi=r1.results[ci]["projp"],
            decf=decz, bbf=bbz, ccs=r1.results[ci]["cco"],
            outwT2=pp["outwT2"], gatew=pp["gatew"], gateb=pp["gateb"],
            identb=pp["identb"]))
    r2 = run_bass_kernel_spmd(nc2, maps2, core_ids=cores)
    astat = np.stack([r2.results[c]["lstato"] for c in cores], axis=2)

    nc3 = build_L3()
    maps3 = []
    for ci in cores:
        maps3.append(dict(
            xn2Ti=r2.results[ci]["xn2To"], logiti=r2.results[ci]["logito"],
            astat=np.ascontiguousarray(astat),
            w1=pp["w1"], w2=pp["w2"], w3=pp["w3"],
            b1c=pp["b1c"], b2c=pp["b2c"], b3r=pp["b3r"], i8=pp["i8"]))
    r3 = run_bass_kernel_spmd(nc3, maps3, core_ids=cores)
    vparts = np.stack([r3.results[c]["vparto"] for c in cores], axis=0)

    nc4 = build_L4()
    maps4 = []
    for ci in cores:
        maps4.append(dict(
            x1i=r2.results[ci]["x1o"], indi=r3.results[ci]["indo"],
            vparts=np.ascontiguousarray(vparts),
            wqT=pp["wqT"], wkT=pp["wkT"], wvT=pp["wvT"],
            bqb=pp["bqb"], bkb=pp["bkb"], bvb=pp["bvb"],
            cosb=pp["cosb"], sinb=pp["sinb"], identb=pp["identb"]))
    r4 = run_bass_kernel_spmd(nc4, maps4, core_ids=cores)

    nc5 = build_L5()
    maps5 = []
    for ci in cores:
        maps5.append(dict(
            x3i=r4.results[ci]["x3o"],
            wg=pp["wgf"], bgb=pp["bgb"], w1=pp["w1f"], b1c=pp["b1f"],
            w2=pp["w2f"], b2r=pp["b2f"], identb=pp["identb"]))
    r5 = run_bass_kernel_spmd(nc5, maps5, core_ids=cores)

    out = np.empty((B, T, D), np.float32)
    for ci in cores:
        out[:, ci * TW:(ci + 1) * TW, :] = r5.results[ci]["outo"]
    return out
